# version 16
# baseline (speedup 1.0000x reference)
import contextlib
import numpy as np
import concourse.bass as bass
import concourse.mybir as mybir
from concourse.bass_utils import run_bass_kernel_spmd

F32 = mybir.dt.float32
AF = mybir.ActivationFunctionType
OP = mybir.AluOpType
AX = mybir.AxisListType.X

B, T, D = 8, 2048, 1024
DI, NS, RK = 2048, 16, 64
KC = 4
E, DE = 8, 512
EPS_LN = 1e-5

CH = 256
NCH = T // CH
NT = CH // 128
NDI = DI // 128
ND = D // 128
G = 4
NG = NS // G

_DEBUG = False
_SKIP = None
_SKIP_GATE = False


class Buf:
    __slots__ = ("ready", "readers", "name")

    def __init__(self, name=""):
        self.ready = None
        self.readers = []
        self.name = name


class MK:
    ENGS = ("Q", "T", "V", "S", "G")
    DSEMS = ("QR", "QX0", "QX1", "QX2", "QX3", "QW0", "QW1", "QO0", "QO1",
             "QA0", "QA1", "QB0", "QB1", "GB0", "GB1", "GR0", "GR1",
             "SB0", "QC0", "ST0", "ST1", "DBG")

    def __init__(self):
        self.cnt = {e: 0 for e in self.ENGS + self.DSEMS}
        self.ops = {e: [] for e in self.ENGS}
        self.waited = {e: {} for e in self.ENGS}
        self.pend_r = {e: [] for e in self.ENGS}
        self.pend_w = {e: [] for e in self.ENGS}

    def emit(self, eng, fn, reads=(), writes=(), inc=True, dma=None,
             vsafe=False):
        waits = {}

        def need(tok, war=False):
            if tok is None:
                return
            te, sem = tok[0], tok[1]
            if tok[3]:
                waits[sem] = max(waits.get(sem, 0), tok[2])
            elif te != eng:
                waits[sem] = max(waits.get(sem, 0), tok[2])
            elif not war and not tok[4] and eng in ("V", "G"):
                waits[sem] = max(waits.get(sem, 0), tok[2])

        for b in reads:
            need(b.ready)
        for b in writes:
            need(b.ready, war=True)
            for t in b.readers:
                need(t, war=True)
        for sem, v in waits.items():
            if self.waited[eng].get(sem, 0) < v:
                self.ops[eng].append(("w", sem, v))
                self.waited[eng][sem] = v
        if inc:
            sem = dma if dma else eng
            amt = 16 if dma else 1
            self.cnt[sem] += amt
            tok = (eng, sem, self.cnt[sem], dma is not None, vsafe)
            self.ops[eng].append(("e", fn, sem, amt))
            for b in list(writes) + self.pend_w[eng]:
                b.ready = tok
                b.readers = []
            for b in list(reads) + self.pend_r[eng]:
                b.readers.append(tok)
            self.pend_r[eng] = []
            self.pend_w[eng] = []
        else:
            self.ops[eng].append(("e", fn, None, 0))
            self.pend_r[eng] += list(reads)
            self.pend_w[eng] += list(writes)

    def final_wait(self, eng, sems_to_wait):
        for sem in sems_to_wait:
            self.ops[eng].append(("w", sem, self.cnt[sem]))

    def fence(self, eng, sem):
        if self.waited[eng].get(sem, 0) < self.cnt[sem]:
            self.ops[eng].append(("w", sem, self.cnt[sem]))
            self.waited[eng][sem] = self.cnt[sem]

    def replay(self, eng, obj, sems):
        for item in self.ops[eng]:
            if item[0] == "w":
                obj.wait_ge(sems[item[1]], item[2])
            else:
                inst = item[1](obj)
                if item[2] is not None:
                    inst.then_inc(sems[item[2]], item[3])


def seg3(buf):
    return buf[:].rearrange("p (n t) -> p n t", n=G)


def segcol(buf, t):
    return seg3(buf)[:, :, t:t + 1].squeeze(2)


def build_program(n_chunks=NCH, debug=False):
    nc = bass.Bass("TRN2", target_bir_lowering=False)

    dx = nc.dram_tensor("x", [T, D], F32, kind="ExternalInput")
    dwin = nc.dram_tensor("w_in_r", [32, 128, 1024], F32, kind="ExternalInput")
    dwcv = nc.dram_tensor("w_conv_r", [128, 4 * NDI], F32, kind="ExternalInput")
    dbcv = nc.dram_tensor("b_conv_r", [128, NDI], F32, kind="ExternalInput")
    dwx = nc.dram_tensor("w_x_r", [128, NDI * 96], F32, kind="ExternalInput")
    dwdt = nc.dram_tensor("w_dt", [RK, DI], F32, kind="ExternalInput")
    dbdt = nc.dram_tensor("b_dt_r", [128, NDI], F32, kind="ExternalInput")
    dA = nc.dram_tensor("A_r", [128, NDI * NS], F32, kind="ExternalInput")
    dD = nc.dram_tensor("D_r", [128, NDI], F32, kind="ExternalInput")
    dwout = nc.dram_tensor("w_out", [DI, D], F32, kind="ExternalInput")
    dw1 = nc.dram_tensor("w1_r", [E, 4, 128, 1024], F32, kind="ExternalInput")
    db1 = nc.dram_tensor("b1_r", [128, E * 4], F32, kind="ExternalInput")
    dw2 = nc.dram_tensor("w2", [E, DE, D], F32, kind="ExternalInput")
    db2 = nc.dram_tensor("b2", [E, D], F32, kind="ExternalInput")
    dcent = nc.dram_tensor("cent", [E, D], F32, kind="ExternalInput")
    dwrt = nc.dram_tensor("w_route", [E, D], F32, kind="ExternalInput")
    dident = nc.dram_tensor("ident", [128, 128], F32, kind="ExternalInput")
    dout = nc.dram_tensor("out", [T, D], F32, kind="ExternalOutput")
    if debug:
        d_xn = nc.dram_tensor("dbg_xnT", [D, T], F32, kind="ExternalOutput")
        d_u = nc.dram_tensor("dbg_uT", [DI, T], F32, kind="ExternalOutput")
        d_dt = nc.dram_tensor("dbg_dtT", [DI, T], F32, kind="ExternalOutput")
        d_y = nc.dram_tensor("dbg_yT", [DI, T], F32, kind="ExternalOutput")
        d_g = nc.dram_tensor("dbg_gates", [T, E], F32, kind="ExternalOutput")
        d_xd = nc.dram_tensor("dbg_xdbl", [96, T], F32, kind="ExternalOutput")
        d_y2 = nc.dram_tensor("dbg_y2T", [DI, T], F32, kind="ExternalOutput")

    def sb(name, p, f):
        return nc.alloc_sbuf_tensor(name, [p, f], F32)

    xsb = [sb(f"x_{tt}", 128, 1024) for tt in range(NT)]
    xres = sb("xres", 128, 1024)
    xn_t = sb("xn_t", 128, 1024)
    xnT = [[sb(f"xnT_{p}_{i}", 128, CH) for i in range(ND)] for p in range(2)]
    u_pad = [sb(f"upad_{i}", 128, CH + 4) for i in range(NDI)]
    uT = [sb(f"uT_{i}", 128, CH) for i in range(NDI)]
    dtb = [sb(f"dt_{i}", 128, CH) for i in range(NDI)]
    yT = [sb(f"yT_{i}", 128, CH) for i in range(NDI)]
    xdbl = sb("xdbl", 96, CH)
    spX = [sb("spX_0", 128, CH)]
    dAg = [sb(f"dAg_{p}", 128, G * CH) for p in range(2)]
    bbg = [sb(f"bbg_{p}", 128, G * CH) for p in range(3)]
    hhg = [sb(f"hhg_{p}", 128, G * CH) for p in range(2)]
    Bint = [sb(f"Bint_{p}", 128, G * CH) for p in range(2)]
    Cint = [sb(f"Cint_{p}", 128, G * CH) for p in range(2)]
    tmpd = [sb(f"tmpd_{p}", 128, G) for p in range(2)]
    hst = sb("hst", 128, NDI * NS)
    szb = [sb(f"sz_{j}", 128, CH) for j in range(2)]
    hgs = [sb(f"hgs_{m}", 128, CH) for m in range(2)]
    grep = [sb(f"grep_{j}", 128, CH) for j in range(2)]
    glog = [sb(f"glog_{tt}", 128, E) for tt in range(NT)]
    gm8 = [sb(f"gm8_{tt}", 128, E) for tt in range(NT)]
    gel = [sb(f"gel_{tt}", 128, E) for tt in range(NT)]
    gkp = [sb(f"gkp_{tt}", 128, E) for tt in range(NT)]
    gnm = [sb(f"gnm_{tt}", 128, 1) for tt in range(NT)]
    gsum = [sb(f"gsum_{tt}", 128, 1) for tt in range(NT)]
    gis = [sb(f"gis_{tt}", 128, 1) for tt in range(NT)]
    gatesT = [nc.alloc_sbuf_tensor(f"gatesT_{p}", [E, CH], F32) for p in range(2)]
    rstd = [sb(f"rstd_{tt}", 128, 1) for tt in range(NT)]
    nmr = [sb(f"nmr_{tt}", 128, 1) for tt in range(NT)]
    ssn = [sb(f"ssn_{tt}", 128, 1) for tt in range(NT)]
    invx = [sb(f"invx_{tt}", 128, 1) for tt in range(NT)]
    wincol = [sb(f"wincol_{j}", 128, 1024) for j in range(2)]
    woutr = [sb(f"woutr_{j}", 128, 1024) for j in range(2)]
    w1t = [sb(f"w1t_{j}", 128, 1024) for j in range(2)]
    w2t = [sb(f"w2t_{j}", 128, 1024) for j in range(2)]
    wdt_s = nc.alloc_sbuf_tensor("wdt_s", [RK, DI], F32)
    wx_s = sb("wx_s", 128, NDI * 96)
    Abuf = sb("Abuf", 128, NDI * NS)
    Dbuf = sb("Dbuf", 128, NDI)
    bcv_s = sb("bcv_s", 128, NDI)
    bdt_s = sb("bdt_s", 128, NDI)
    wcv_s = sb("wcv_s", 128, 4 * NDI)
    b1c_s = sb("b1c_s", 128, E * 4)
    b2_s = nc.alloc_sbuf_tensor("b2_s", [E, D], F32)
    cns = [nc.alloc_sbuf_tensor(f"cns_{j}", [E, 1], F32) for j in range(3)]
    cnrT = sb("cnrT", 128, ND * E)
    wrtT = sb("wrtT", 128, ND * E)
    ident_s = sb("ident_s", 128, 128)
    cst = sb("cst", 128, 3)

    psA = nc.alloc_psum_tensor("psA", [128, 512], F32)
    psF = [nc.alloc_psum_tensor(f"psF_{j}", [128, 512], F32) for j in range(2)]
    psO = [nc.alloc_psum_tensor(f"psO_{tt}", [128, 1024], F32) for tt in range(NT)]
    psZ = nc.alloc_psum_tensor("psZ", [128, 512], F32)

    mk = MK()
    b_x = [Buf(f"x{tt}") for tt in range(NT)]
    b_xres = Buf("xres")
    b_xn = Buf("xn")
    b_xnT = [[Buf(f"xnT{p}{i}") for i in range(ND)] for p in range(2)]
    b_upad = [Buf(f"upad{i}") for i in range(NDI)]
    b_uT = [Buf(f"uT{i}") for i in range(NDI)]
    b_dt = [Buf(f"dt{i}") for i in range(NDI)]
    b_yT = [Buf(f"yT{i}") for i in range(NDI)]
    b_xdbl = Buf("xdbl")
    b_spX = [Buf("spX0")]
    b_dAg = [Buf(f"dAg{p}") for p in range(2)]
    b_bbg = [Buf(f"bbg{p}") for p in range(3)]
    b_hhg = [Buf(f"hhg{p}") for p in range(2)]
    b_Bint = [Buf(f"Bint{p}") for p in range(2)]
    b_Cint = [Buf(f"Cint{p}") for p in range(2)]
    b_BintS = Buf("BintS")
    b_CintQ = Buf("CintQ")
    b_CintG = Buf("CintG")
    b_tmpd = [Buf(f"tmpd{p}") for p in range(2)]
    b_hst = [[Buf(f"hst{i}_{g}") for g in range(NG)] for i in range(NDI)]
    b_sz = [Buf(f"sz{j}") for j in range(2)]
    b_hgs = [Buf(f"hgs{m}") for m in range(2)]
    b_grep = [Buf(f"grep{j}") for j in range(2)]
    b_glog = [Buf(f"glog{tt}") for tt in range(NT)]
    b_gm8 = [Buf(f"gm8{tt}") for tt in range(NT)]
    b_gel = [Buf(f"gel{tt}") for tt in range(NT)]
    b_gkp = [Buf(f"gkp{tt}") for tt in range(NT)]
    b_gnm = [Buf(f"gnm{tt}") for tt in range(NT)]
    b_gsum = [Buf(f"gsum{tt}") for tt in range(NT)]
    b_gis = [Buf(f"gis{tt}") for tt in range(NT)]
    b_gatesT = [Buf(f"gatesT{p}") for p in range(2)]
    b_rstd = [Buf(f"rstd{tt}") for tt in range(NT)]
    b_nmr = [Buf(f"nmr{tt}") for tt in range(NT)]
    b_ssn = [Buf(f"ssn{tt}") for tt in range(NT)]
    b_invx = [Buf(f"invx{tt}") for tt in range(NT)]
    b_wincol = [Buf(f"wincol{j}") for j in range(2)]
    b_woutr = [Buf(f"woutr{j}") for j in range(2)]
    b_w1t = [Buf(f"w1t{j}") for j in range(2)]
    b_w2t = [Buf(f"w2t{j}") for j in range(2)]
    b_res = Buf("residents")
    b_cst = Buf("cst")
    b_psA = Buf("psA")
    b_psF = [Buf(f"psF{j}") for j in range(2)]
    b_psO = [Buf(f"psO{tt}") for tt in range(NT)]
    b_psZ = Buf("psZ")

    def ld(dst_ap, src_ap, bufs_w, sem="QR"):
        mk.emit("Q", lambda q, d=dst_ap, s=src_ap: q.dma_start(out=d, in_=s),
                reads=(), writes=bufs_w, dma=sem)

    ld(wdt_s[:], dwdt.ap(), [b_res])
    ld(wx_s[:], dwx.ap(), [b_res])
    ld(Abuf[:], dA.ap(), [b_res])
    ld(Dbuf[:], dD.ap(), [b_res])
    ld(bcv_s[:], dbcv.ap(), [b_res])
    ld(bdt_s[:], dbdt.ap(), [b_res])
    ld(wcv_s[:], dwcv.ap(), [b_res])
    ld(b1c_s[:], db1.ap(), [b_res])
    ld(b2_s[:], db2.ap(), [b_res])
    cent_s, wrt_s = w1t[0], w1t[1]
    ld(cent_s[0:E, :], dcent.ap(), [b_w1t[0]])
    ld(wrt_s[0:E, :], dwrt.ap(), [b_w1t[1]])
    ld(ident_s[:], dident.ap(), [b_res])

    mk.emit("V", lambda v: v.memset(cst[:, 0:1], EPS_LN), writes=[b_cst])
    mk.emit("V", lambda v: v.memset(cst[:, 1:2], -1.0), writes=[b_cst])
    mk.emit("V", lambda v: v.memset(cst[:, 2:3], 1.0 / 1024.0), writes=[b_cst])

    mk.emit("S", lambda s: s.activation(Abuf[:], Abuf[:], AF.Exp),
            reads=[b_res], writes=[b_res])
    mk.emit("V", lambda v: v.tensor_scalar_mul(Abuf[:], Abuf[:], cst[:, 1:2]),
            reads=[b_res, b_cst], writes=[b_res])
    mk.emit("V", lambda v: v.memset(hst[:], 0.0),
            writes=[bb for row in b_hst for bb in row])
    for p in range(2):
        mk.emit("V", lambda v, p=p: v.memset(dAg[p][:], 0.0), writes=[b_dAg[p]])
    for i in range(NDI):
        mk.emit("V", lambda v, i=i: v.memset(u_pad[i][:, 0:4], 0.0),
                writes=[b_upad[i]])
    mk.emit("S", lambda s: s.activation(w2t[0][0:E, :], cent_s[0:E, :],
                                        AF.Square, accum_out=cns[0][:]),
            reads=[b_w1t[0]], writes=[b_w2t[0], b_res])
    mk.emit("S", lambda s: s.activation(cns[1][:], cns[0][:], AF.Sqrt),
            reads=[b_res], writes=[b_res])
    mk.emit("V", lambda v: v.tensor_scalar_max(cns[1][:], cns[1][:], 1e-12),
            reads=[b_res], writes=[b_res])
    mk.emit("V", lambda v: v.reciprocal(cns[2][:], cns[1][:]),
            reads=[b_res], writes=[b_res])
    mk.emit("S", lambda s: s.mul(cent_s[0:E, :], cent_s[0:E, :], cns[2][:]),
            reads=[b_res, b_w1t[0]], writes=[b_w1t[0]])
    for srct, bsrc, dstT in ((cent_s, b_w1t[0], cnrT), (wrt_s, b_w1t[1], wrtT)):
        for j in range(ND):
            mk.emit("T", lambda t, s=srct, j=j: t.transpose(
                psA[:, 0:E], s[0:E, j * 128:(j + 1) * 128],
                ident_s[0:E, 0:E]), reads=[b_res, bsrc], writes=[b_psA])
            mk.emit("S", lambda s, d=dstT, j=j: s.copy(
                d[:, j * E:(j + 1) * E], psA[:, 0:E]),
                reads=[b_psA], writes=[b_res])

    def load_x(k):
        for tt in range(NT):
            r0 = k * CH + tt * 128
            ld(xsb[tt][:], dx.ap()[r0:r0 + 128, :], [b_x[tt]],
               sem=f"QX{tt}")

    load_x(0)

    scan_it = [0]
    NB_ = NDI // 4

    def bcast_g0_spread():
        for jj in range(2):
            mk.emit("G", lambda g_, jj=jj: g_.dma_start(
                out=Bint[0][:, jj * CH:(jj + 1) * CH],
                in_=xdbl[RK + jj:RK + jj + 1, :].unsqueeze(1)
                .to_broadcast([1, 128, CH])),
                reads=[b_xdbl], writes=[b_Bint[0]], dma="GB0")
            mk.emit("S", lambda s, jj=jj: s.dma_start(
                out=Bint[0][:, (jj + 2) * CH:(jj + 3) * CH],
                in_=xdbl[RK + jj + 2:RK + jj + 3, :].unsqueeze(1)
                .to_broadcast([1, 128, CH])),
                reads=[b_xdbl], writes=[b_BintS], dma="SB0")
        for jj in range(3):
            mk.emit("Q", lambda q, jj=jj: q.dma_start(
                out=Cint[0][:, jj * CH:(jj + 1) * CH],
                in_=xdbl[RK + NS + jj:RK + NS + jj + 1, :].unsqueeze(1)
                .to_broadcast([1, 128, CH])),
                reads=[b_xdbl], writes=[b_CintQ], dma="QC0")
        mk.emit("G", lambda g_: g_.dma_start(
            out=Cint[0][:, 3 * CH:4 * CH],
            in_=xdbl[RK + NS + 3:RK + NS + 4, :].unsqueeze(1)
            .to_broadcast([1, 128, CH])),
            reads=[b_xdbl], writes=[b_CintG], dma="GB0")

    def head_steps_early(kn):
        kp = kn % 2
        xnTk = xnT[kp]
        b_xnTk = b_xnT[kp]

        for tt in range(NT):
            xt = xsb[tt]
            bx = b_x[tt]
            mk.emit("V", lambda v, xt=xt, tt=tt: v.reduce_sum(
                ssn[tt][:], xt[:], AX), reads=[bx], writes=[b_ssn[tt]])
            mk.emit("V", lambda v, tt=tt: v.tensor_scalar_mul(
                nmr[tt][:], ssn[tt][:], -1.0 / 1024.0),
                reads=[b_ssn[tt]], writes=[b_nmr[tt]])
            yield
            mk.emit("S", lambda s, xt=xt, tt=tt: s.activation(
                xn_t[:], xt[:], AF.Identity, bias=nmr[tt][:]),
                reads=[bx, b_nmr[tt]], writes=[b_xn])
            mk.emit("S", lambda s, tt=tt: s.activation(
                xn_t[:], xn_t[:], AF.Square, accum_out=ssn[tt][:]),
                reads=[b_xn], writes=[b_xn, b_ssn[tt]])
            mk.emit("S", lambda s, tt=tt: s.activation(
                rstd[tt][:], ssn[tt][:], AF.Sqrt, bias=cst[:, 0:1],
                scale=cst[:, 2:3]),
                reads=[b_ssn[tt], b_cst], writes=[b_rstd[tt]])
            yield
            mk.emit("V", lambda v, tt=tt: v.reciprocal(rstd[tt][:], rstd[tt][:]),
                    reads=[b_rstd[tt]], writes=[b_rstd[tt]])
            mk.emit("V", lambda v, tt=tt: v.tensor_mul(
                nmr[tt][:], nmr[tt][:], rstd[tt][:]),
                reads=[b_nmr[tt], b_rstd[tt]], writes=[b_nmr[tt]])
            yield
            mk.emit("S", lambda s, xt=xt, tt=tt: s.activation(
                xn_t[:], xt[:], AF.Identity, bias=nmr[tt][:], scale=rstd[tt][:]),
                reads=[bx, b_nmr[tt], b_rstd[tt]], writes=[b_xn])
            if kn + 1 < n_chunks:
                r0 = (kn + 1) * CH + tt * 128
                ld(xsb[tt][:], dx.ap()[r0:r0 + 128, :], [b_x[tt]],
                   sem=f"QX{tt}")
            yield
            for i in range(ND):
                sl = i % 4
                mk.emit("T", lambda t, i=i, sl=sl: t.transpose(
                    psA[:, sl * 128:(sl + 1) * 128],
                    xn_t[:, i * 128:(i + 1) * 128], ident_s[:]),
                    reads=[b_xn, b_res], writes=[b_psA])
                mk.emit("S", lambda s, i=i, tt=tt, sl=sl: s.copy(
                    xnTk[i][:, tt * 128:(tt + 1) * 128],
                    psA[:, sl * 128:(sl + 1) * 128]),
                    reads=[b_psA], writes=[b_xnTk[i]])
                if i == 3:
                    yield
            yield
            mk.emit("S", lambda s, tt=tt: s.activation(
                xn_t[:], xn_t[:], AF.Square, accum_out=ssn[tt][:]),
                reads=[b_xn], writes=[b_xn, b_ssn[tt]])
            mk.emit("S", lambda s, tt=tt: s.activation(
                ssn[tt][:], ssn[tt][:], AF.Sqrt),
                reads=[b_ssn[tt]], writes=[b_ssn[tt]])
            mk.emit("V", lambda v, tt=tt: v.tensor_scalar_max(
                ssn[tt][:], ssn[tt][:], 1e-12),
                reads=[b_ssn[tt]], writes=[b_ssn[tt]])
            mk.emit("V", lambda v, tt=tt: v.reciprocal(invx[tt][:], ssn[tt][:]),
                    reads=[b_ssn[tt]], writes=[b_invx[tt]])
            yield

        if debug:
            c0n = kn * CH
            for i in range(ND):
                mk.emit("G", lambda g, i=i, c0n=c0n: g.dma_start(
                    out=d_xn.ap()[i * 128:(i + 1) * 128, c0n:c0n + CH],
                    in_=xnTk[i][:]), reads=[b_xnTk[i]], dma="DBG")

        for tt in range(NT):
            for kd in range(ND):
                mk.emit("T", lambda t, tt=tt, kd=kd: t.matmul(
                    psA[0:128, 0:E],
                    xnTk[kd][:, tt * 128:(tt + 1) * 128],
                    cnrT[:, kd * E:(kd + 1) * E],
                    start=(kd == 0), stop=(kd == ND - 1)),
                    reads=[b_xnTk[kd], b_res], writes=[b_psA],
                    inc=(kd == ND - 1))
            for kd in range(ND):
                mk.emit("T", lambda t, tt=tt, kd=kd: t.matmul(
                    psA[0:128, E:2 * E],
                    xnTk[kd][:, tt * 128:(tt + 1) * 128],
                    wrtT[:, kd * E:(kd + 1) * E],
                    start=(kd == 0), stop=(kd == ND - 1)),
                    reads=[b_xnTk[kd], b_res], writes=[b_psA],
                    inc=(kd == ND - 1))
            mk.emit("S", lambda s, tt=tt: s.copy(glog[tt][:], psA[:, 0:E]),
                    reads=[b_psA], writes=[b_glog[tt]])
            mk.emit("V", lambda v, tt=tt: v.scalar_tensor_tensor(
                glog[tt][:], glog[tt][:], invx[tt][:], psA[:, E:2 * E],
                OP.mult, OP.add),
                reads=[b_psA, b_glog[tt], b_invx[tt]], writes=[b_glog[tt]])
            mk.emit("V", lambda v, tt=tt: v.max(gm8[tt][:], glog[tt][:]),
                    reads=[b_glog[tt]], writes=[b_gm8[tt]])
            mk.emit("V", lambda v, tt=tt: v.tensor_scalar_mul(
                gnm[tt][:], gm8[tt][:, 0:1], -1.0),
                reads=[b_gm8[tt]], writes=[b_gnm[tt]])
            mk.emit("S", lambda s, tt=tt: s.activation(
                gel[tt][:], glog[tt][:], AF.Exp, bias=gnm[tt][:]),
                reads=[b_glog[tt], b_gnm[tt]], writes=[b_gel[tt]])
            mk.emit("V", lambda v, tt=tt: v.tensor_single_scalar(
                gkp[tt][:], glog[tt][:], gm8[tt][:, 1:2], OP.is_ge),
                reads=[b_glog[tt], b_gm8[tt]], writes=[b_gkp[tt]])
            mk.emit("V", lambda v, tt=tt: v.tensor_mul(
                gel[tt][:], gel[tt][:], gkp[tt][:]),
                reads=[b_gel[tt], b_gkp[tt]], writes=[b_gel[tt]])
            mk.emit("V", lambda v, tt=tt: v.reduce_sum(
                gsum[tt][:], gel[tt][:], AX),
                reads=[b_gel[tt]], writes=[b_gsum[tt]])
            mk.emit("V", lambda v, tt=tt: v.reciprocal(gis[tt][:], gsum[tt][:]),
                    reads=[b_gsum[tt]], writes=[b_gis[tt]])
            mk.emit("V", lambda v, tt=tt: v.tensor_scalar_mul(
                gel[tt][:], gel[tt][:], gis[tt][:]),
                reads=[b_gel[tt], b_gis[tt]], writes=[b_gel[tt]])
            if debug:
                c0n = kn * CH
                mk.emit("G", lambda g, tt=tt, c0n=c0n: g.dma_start(
                    out=d_g.ap()[c0n + tt * 128:c0n + (tt + 1) * 128, :],
                    in_=gel[tt][:]), reads=[b_gel[tt]], dma="DBG")
            mk.emit("T", lambda t, tt=tt: t.transpose(
                psA[0:E, 256 + tt * 128:256 + (tt + 1) * 128],
                gel[tt][:], ident_s[:]),
                reads=[b_gel[tt], b_res], writes=[b_psA])
            mk.emit("S", lambda s, tt=tt, kp=kp: s.copy(
                gatesT[kp][:, tt * 128:(tt + 1) * 128],
                psA[0:E, 256 + tt * 128:256 + (tt + 1) * 128]),
                reads=[b_psA], writes=[b_gatesT[kp]])
            yield


    def head_late_steps(kn):
        kp = kn % 2
        xnTk = xnT[kp]
        b_xnTk = b_xnT[kp]

        for m in range(NDI):
            j = m % 2
            ld(wincol[j][:], dwin.ap()[m, :, :], [b_wincol[j]], sem=f"QW{j}")
            for kd in range(8):
                mk.emit("T", lambda t, j=j, kd=kd: t.matmul(
                    psF[j][:, 0:CH],
                    wincol[j][:, kd * 128:(kd + 1) * 128],
                    xnTk[kd][:], start=(kd == 0), stop=(kd == 7)),
                    reads=[b_wincol[j], b_xnTk[kd]], writes=[b_psF[j]],
                    inc=(kd == 7))
            mk.emit("S", lambda s, m=m, j=j: s.copy(
                u_pad[m][:, 4:4 + CH], psF[j][:, 0:CH]),
                reads=[b_psF[j]], writes=[b_upad[m]])
            if m % 2 == 1:
                for mm in (m - 1, m):
                    mk.emit("V", lambda v, m=mm: v.tensor_scalar_mul(
                        uT[m][:], u_pad[m][:, 4:4 + CH],
                        wcv_s[:, m * 4 + 3:m * 4 + 4]),
                        reads=[b_upad[m], b_res], writes=[b_uT[m]],
                        vsafe=True)
                for kk in range(3):
                    for mm in (m - 1, m):
                        mk.emit("V", lambda v, m=mm, kk=kk:
                                v.scalar_tensor_tensor(
                            uT[m][:], u_pad[m][:, 1 + kk:1 + kk + CH],
                            wcv_s[:, m * 4 + kk:m * 4 + kk + 1], uT[m][:],
                            OP.mult, OP.add),
                            reads=[b_upad[m], b_res, b_uT[m]],
                            writes=[b_uT[m]], vsafe=True)
                for mm in (m - 1, m):
                    mk.emit("S", lambda s, m=mm: s.copy(
                        u_pad[m][:, 0:4], u_pad[m][:, CH:CH + 4]),
                        reads=[b_upad[m]], writes=[b_upad[m]])
                    mk.emit("S", lambda s, m=mm: s.activation(
                        uT[m][:], uT[m][:], AF.Silu, bias=bcv_s[:, m:m + 1]),
                        reads=[b_uT[m], b_res], writes=[b_uT[m]])
            yield

        for kd in range(NDI):
            mk.emit("T", lambda t, kd=kd: t.matmul(
                psZ[0:96, 0:CH], wx_s[:, kd * 96:(kd + 1) * 96],
                uT[kd][:], start=(kd == 0), stop=(kd == NDI - 1)),
                reads=[b_res, b_uT[kd]], writes=[b_psZ],
                inc=(kd == NDI - 1))
        mk.emit("S", lambda s: s.copy(xdbl[:], psZ[0:96, 0:CH]),
                reads=[b_psZ], writes=[b_xdbl])
        if debug:
            c0n = kn * CH
            mk.emit("G", lambda g, c0n=c0n: g.dma_start(
                out=d_xd.ap()[:, c0n:c0n + CH], in_=xdbl[:]),
                reads=[b_xdbl], dma="DBG")

        for i in range(NDI):
            j = i % 2
            mk.emit("T", lambda t, i=i, j=j: t.matmul(
                psF[j][:, 0:CH], wdt_s[:, i * 128:(i + 1) * 128],
                xdbl[0:RK, :], start=True, stop=True),
                reads=[b_res, b_xdbl], writes=[b_psF[j]])
            mk.emit("S", lambda s, i=i, j=j: s.activation(
                spX[0][:], psF[j][:, 0:CH], AF.Abs, bias=bdt_s[:, i:i + 1]),
                reads=[b_psF[j], b_res], writes=[b_spX[0]])
            mk.emit("S", lambda s: s.activation(
                spX[0][:], spX[0][:], AF.Exp, scale=cst[:, 1:2]),
                reads=[b_spX[0], b_cst], writes=[b_spX[0]])
            mk.emit("S", lambda s: s.activation(
                spX[0][:], spX[0][:], AF.Ln, bias=1.0),
                reads=[b_spX[0]], writes=[b_spX[0]])
            mk.emit("S", lambda s, i=i, j=j: s.activation(
                dtb[i][:], psF[j][:, 0:CH], AF.Relu, bias=bdt_s[:, i:i + 1]),
                reads=[b_psF[j], b_res], writes=[b_dt[i]])
            mk.emit("V", lambda v, i=i: v.tensor_add(
                dtb[i][:], dtb[i][:], spX[0][:]),
                reads=[b_spX[0], b_dt[i]], writes=[b_dt[i]], vsafe=True)
        for i in range(NDI):
            mk.emit("V", lambda v, i=i: v.tensor_mul(
                u_pad[i][:, 4:4 + CH], dtb[i][:], uT[i][:]),
                reads=[b_dt[i], b_uT[i]], writes=[b_upad[i]], vsafe=True)
        yield
        bcast_g0_spread()

        if debug:
            c0n = kn * CH
            for i in range(NDI):
                mk.emit("G", lambda g, i=i, c0n=c0n: g.dma_start(
                    out=d_u.ap()[i * 128:(i + 1) * 128, c0n:c0n + CH],
                    in_=uT[i][:]), reads=[b_uT[i]], dma="DBG")
                mk.emit("G", lambda g, i=i, c0n=c0n: g.dma_start(
                    out=d_dt.ap()[i * 128:(i + 1) * 128, c0n:c0n + CH],
                    in_=dtb[i][:]), reads=[b_dt[i]], dma="DBG")


    def emit_scanphase(k, feeder, late_feeder, bcast0_done):
        kp = k % 2
        c0 = k * CH
        xnTk = xnT[kp]
        b_xnTk = b_xnT[kp]

        def feed(n):
            for _ in range(n):
                if next(feeder, "END") == "END":
                    break

        def ffn_stages(e):
            h = e % 2
            mk.emit("G", lambda g, e=e, h=h, kp=kp: g.dma_start(
                out=grep[h][:],
                in_=gatesT[kp][e:e + 1, :].unsqueeze(1)
                .to_broadcast([1, 128, CH])),
                reads=[b_gatesT[kp]], writes=[b_grep[h]], dma=f"GR{h}")

            def h_stage(m):
                j = (e * 4 + m) % 2
                hj = m % 2
                ld(w1t[j][:], dw1.ap()[e, m, :, :], [b_w1t[j]], sem=f"QA{j}")
                for kd in range(8):
                    mk.emit("T", lambda t, j=j, kd=kd: t.matmul(
                        psF[j][:, 0:CH], w1t[j][:, kd * 128:(kd + 1) * 128],
                        xnTk[kd][:], start=(kd == 0), stop=(kd == 7)),
                        reads=[b_w1t[j], b_xnTk[kd]], writes=[b_psF[j]],
                        inc=(kd == 7))
                mk.emit("S", lambda s, e=e, m=m, j=j, hj=hj: s.activation(
                    hgs[hj][:], psF[j][:, 0:CH], AF.Gelu,
                    bias=b1c_s[:, e * 4 + m:e * 4 + m + 1]),
                    reads=[b_psF[j], b_res], writes=[b_hgs[hj]])
                mk.emit("G", lambda g_, hj=hj, h=h: g_.tensor_tensor(
                    hgs[hj][:], hgs[hj][:], grep[h][:], OP.mult),
                    reads=[b_hgs[hj], b_grep[h]], writes=[b_hgs[hj]])

            def w2_stage(m):
                wj = m % 2
                hj = m % 2
                ld(w2t[wj][:], dw2.ap()[e, m * 128:(m + 1) * 128, :],
                   [b_w2t[wj]], sem=f"QB{wj}")
                for tt in range(NT):
                    for nh in range(2):
                        mk.emit("T", lambda t, wj=wj, hj=hj, tt=tt, nh=nh,
                                e=e, m=m: t.matmul(
                            psO[tt][:, nh * 512:(nh + 1) * 512],
                            hgs[hj][:, tt * 128:(tt + 1) * 128],
                            w2t[wj][:, nh * 512:(nh + 1) * 512],
                            start=(e == 0 and m == 0), stop=False),
                            reads=[b_w2t[wj], b_hgs[hj]], writes=[b_psO[tt]],
                            inc=(tt == NT - 1 and nh == 1))

            return [lambda m=m: h_stage(m) if m >= 0 else w2_stage(-1 - m)
                    for m in (0, 1, -1, 2, -2, 3, -3, -4)]

        pend = [None, None]

        def emit_scan_core(i, g, pp, ap2, gp):
            mk.emit("V", lambda v, pp=pp, ap2=ap2: v.tensor_tensor_scan(
                hhg[ap2][:], dAg[ap2][:], bbg[pp][:], 0.0, OP.mult, OP.add),
                reads=[b_dAg[ap2], b_bbg[pp]], writes=[b_hhg[ap2]],
                vsafe=True)
            mk.emit("G", lambda g_, i=i, g=g, ap2=ap2: g_.tensor_copy(
                hst[:, i * NS + g * G:i * NS + (g + 1) * G],
                segcol(hhg[ap2], CH - 1)),
                reads=[b_hhg[ap2]], writes=[b_hst[i][g]])

        def emit_ytail(i, g, pp, ap2, gp):
            mk.emit("V", lambda v, pp=pp, ap2=ap2, gp=gp: v.tensor_tensor(
                bbg[pp][:], hhg[ap2][:], Cint[gp][:], OP.mult),
                reads=[b_hhg[ap2], b_Cint[gp]]
                + ([b_CintQ, b_CintG] if gp == 0 else []),
                writes=[b_bbg[pp]], vsafe=True)
            mk.emit("V", lambda v, pp=pp: v.tensor_add(
                bbg[pp][:, 0:2 * CH], bbg[pp][:, 0:2 * CH],
                bbg[pp][:, 2 * CH:4 * CH]),
                reads=[b_bbg[pp]], writes=[b_bbg[pp]], vsafe=True)
            if g == 0:
                mk.emit("V", lambda v, i=i, pp=pp: v.tensor_add(
                    yT[i][:], bbg[pp][:, 0:CH], bbg[pp][:, CH:2 * CH]),
                    reads=[b_bbg[pp]], writes=[b_yT[i]], vsafe=True)
            else:
                mk.emit("V", lambda v, pp=pp: v.tensor_add(
                    bbg[pp][:, 0:CH], bbg[pp][:, 0:CH], bbg[pp][:, CH:2 * CH]),
                    reads=[b_bbg[pp]], writes=[b_bbg[pp]], vsafe=True)
                mk.emit("V", lambda v, i=i, pp=pp: v.tensor_add(
                    yT[i][:], yT[i][:], bbg[pp][:, 0:CH]),
                    reads=[b_bbg[pp], b_yT[i]], writes=[b_yT[i]], vsafe=True)

        def advance(cur):
            if pend[1] is not None:
                emit_ytail(*pend[1])
            pend[1] = pend[0]
            pend[0] = cur

        ffn_iter = iter(range(E))

        def zy2(i):
            m = NDI + i
            j = m % 2
            ld(wincol[j][:], dwin.ap()[m, :, :], [b_wincol[j]], sem=f"QW{j}")
            for kd in range(8):
                mk.emit("T", lambda t, j=j, kd=kd: t.matmul(
                    psZ[:, 0:CH], wincol[j][:, kd * 128:(kd + 1) * 128],
                    xnTk[kd][:], start=(kd == 0), stop=(kd == 7)),
                    reads=[b_wincol[j], b_xnTk[kd]], writes=[b_psZ],
                    inc=(kd == 7))
            mk.emit("S", lambda s, i=i: s.activation(
                szb[i % 2][:], psZ[:, 0:CH], AF.Silu),
                reads=[b_psZ], writes=[b_sz[i % 2]])
            mk.emit("V", lambda v, i=i: v.scalar_tensor_tensor(
                yT[i][:], uT[i][:], Dbuf[:, i:i + 1], yT[i][:],
                OP.mult, OP.add),
                reads=[b_uT[i], b_res, b_yT[i]], writes=[b_yT[i]],
                vsafe=True)
            mk.emit("V", lambda v, i=i: v.tensor_mul(
                yT[i][:], yT[i][:], szb[i % 2][:]),
                reads=[b_yT[i], b_sz[i % 2]], writes=[b_yT[i]], vsafe=True)

        zy2_done = [0]

        def feed_late(n):
            for _ in range(n):
                if next(late_feeder, "END") == "END":
                    break

        def bc_bcast(g):
            gp = g % 2
            for jj in range(G):
                n = g * G + jj
                mk.emit("G", lambda g_, n=n, jj=jj, gp=gp: g_.dma_start(
                    out=Bint[gp][:, jj * CH:(jj + 1) * CH],
                    in_=xdbl[RK + n:RK + n + 1, :].unsqueeze(1)
                    .to_broadcast([1, 128, CH])),
                    reads=[b_xdbl], writes=[b_Bint[gp]], dma=f"GB{gp}")
                mk.emit("G", lambda g_, n=n, jj=jj, gp=gp: g_.dma_start(
                    out=Cint[gp][:, jj * CH:(jj + 1) * CH],
                    in_=xdbl[RK + NS + n:RK + NS + n + 1, :].unsqueeze(1)
                    .to_broadcast([1, 128, CH])),
                    reads=[b_xdbl], writes=[b_Cint[gp]], dma=f"GB{gp}")

        stageq = []

        for g in range(NG):
            gp = g % 2
            if g == 0 and not bcast0_done:
                bcast_g0_spread()
            for _ in range(2):
                e = next(ffn_iter, None)
                if e is not None:
                    stageq.extend(ffn_stages(e))
            for i in range(NDI):
                pp = scan_it[0] % 3
                ap2 = scan_it[0] % 2
                scan_it[0] += 1
                cur = (i, g, pp, ap2, gp)
                for jj in range(G):
                    n = g * G + jj
                    mk.emit("S", lambda s, i=i, jj=jj, n=n, ap2=ap2:
                            s.activation(
                        seg3(dAg[ap2])[:, jj:jj + 1, 1:CH].squeeze(1),
                        dtb[i][:, 1:CH], AF.Exp,
                        scale=Abuf[:, i * NS + n:i * NS + n + 1]),
                        reads=[b_dt[i], b_res], writes=[b_dAg[ap2]],
                        inc=(jj == G - 1))
                tp = scan_it[0] % 2
                mk.emit("S", lambda s, i=i, g=g, tp=tp: s.mul(
                    tmpd[tp][:],
                    Abuf[:, i * NS + g * G:i * NS + (g + 1) * G],
                    dtb[i][:, 0:1]),
                    reads=[b_dt[i], b_res], writes=[b_tmpd[tp]])
                mk.emit("S", lambda s, tp=tp: s.activation(
                    tmpd[tp][:], tmpd[tp][:], AF.Exp),
                    reads=[b_tmpd[tp]], writes=[b_tmpd[tp]])
                mk.emit("G", lambda g_, i=i, g=g, tp=tp: g_.tensor_tensor(
                    tmpd[tp][:], tmpd[tp][:],
                    hst[:, i * NS + g * G:i * NS + (g + 1) * G], OP.mult),
                    reads=[b_tmpd[tp], b_hst[i][g]], writes=[b_tmpd[tp]])
                mk.emit("V", lambda v, i=i, pp=pp, gp=gp: v.tensor_tensor(
                    seg3(bbg[pp]),
                    u_pad[i][:, 4:4 + CH].unsqueeze(1)
                    .to_broadcast([128, G, CH]),
                    seg3(Bint[gp]), OP.mult),
                    reads=[b_upad[i], b_Bint[gp]]
                    + ([b_BintS] if gp == 0 else []),
                    writes=[b_bbg[pp]], vsafe=True)
                mk.emit("G", lambda g_, pp=pp, tp=tp: g_.tensor_tensor(
                    segcol(bbg[pp], 0), segcol(bbg[pp], 0), tmpd[tp][:],
                    OP.add),
                    reads=[b_tmpd[tp], b_bbg[pp]], writes=[b_bbg[pp]])
                if stageq:
                    stageq.pop(0)()
                if i % 2 == 0 and stageq:
                    stageq.pop(0)()
                if pend[0] is not None:
                    emit_scan_core(*pend[0])
                advance(cur)
                if i == 2 and g + 1 < NG:
                    bc_bcast(g + 1)
                if g < NG - 1:
                    if i % 2 == 1:
                        feed(1)
                else:
                    if i >= 2:
                        zy2(i - 2)
                        zy2_done[0] = i - 1
                        if i % 2 == 1:
                            feed_late(1)
        if pend[0] is not None:
            emit_scan_core(*pend[0])
        if pend[1] is not None:
            emit_ytail(*pend[1])
        if pend[0] is not None:
            emit_ytail(*pend[0])
        for e in ffn_iter:
            stageq.extend(ffn_stages(e))
        for th in stageq:
            th()
        stageq.clear()
        for i in range(zy2_done[0], NDI):
            zy2(i)
        feed(1000)
        feed_late(1000)

        if debug:
            for i in range(NDI):
                mk.emit("G", lambda g, i=i, c0=c0: g.dma_start(
                    out=d_y2.ap()[i * 128:(i + 1) * 128, c0:c0 + CH],
                    in_=yT[i][:]), reads=[b_yT[i]], dma="DBG")

    def emit_tail(k):
        par = k % 2
        kp = k % 2
        c0 = k * CH

        for kd in range(NDI):
            j = kd % 2
            ld(woutr[j][:], dwout.ap()[kd * 128:(kd + 1) * 128, :],
               [b_woutr[j]], sem=f"QO{j}")
            for tt in range(NT):
                for nh in range(2):
                    mk.emit("T", lambda t, j=j, kd=kd, tt=tt, nh=nh: t.matmul(
                        psO[tt][:, nh * 512:(nh + 1) * 512],
                        yT[kd][:, tt * 128:(tt + 1) * 128],
                        woutr[j][:, nh * 512:(nh + 1) * 512],
                        start=False, stop=False),
                        reads=[b_woutr[j], b_yT[kd]], writes=[b_psO[tt]],
                        inc=(tt == NT - 1 and nh == 1))

        for tt in range(NT):
            for nh in range(2):
                mk.emit("T", lambda t, tt=tt, nh=nh, kp=kp: t.matmul(
                    psO[tt][:, nh * 512:(nh + 1) * 512],
                    gatesT[kp][:, tt * 128:(tt + 1) * 128],
                    b2_s[:, nh * 512:(nh + 1) * 512],
                    start=False, stop=True),
                    reads=[b_gatesT[kp], b_res], writes=[b_psO[tt]],
                    inc=(nh == 1))

        for tt in range(NT):
            r0 = c0 + tt * 128
            ld(xres[:], dx.ap()[r0:r0 + 128, :], [b_xres], sem="QX2")
            mk.emit("V", lambda v, tt=tt: v.tensor_add(
                xres[:], psO[tt][:], xres[:]),
                reads=[b_psO[tt], b_xres], writes=[b_xres], vsafe=True)
            mk.emit("G", lambda g, tt=tt, c0=c0: g.dma_start(
                out=dout.ap()[c0 + tt * 128:c0 + (tt + 1) * 128, :],
                in_=xres[:]), reads=[b_xres], dma=f"ST{k % 2}")

    for _ in head_steps_early(0):
        pass
    for _ in head_late_steps(0):
        pass
    for k in range(n_chunks):
        feeder = head_steps_early(k + 1) if k + 1 < n_chunks else iter(())
        late_feeder = head_late_steps(k + 1) if k + 1 < n_chunks else iter(())
        emit_scanphase(k, feeder, late_feeder, bcast0_done=(k > 0))
        emit_tail(k)

    mk.final_wait("G", ["ST0", "ST1", "DBG"])

    es = contextlib.ExitStack()
    sems = {e: es.enter_context(nc.semaphore(f"sem{e}"))
            for e in MK.ENGS + MK.DSEMS}
    with nc.Block() as block:
        @block.sync
        def _(sync):
            mk.replay("Q", sync, sems)

        @block.tensor
        def _(tensor):
            mk.replay("T", tensor, sems)

        @block.vector
        def _(vector):
            mk.replay("V", vector, sems)

        @block.scalar
        def _(scalar):
            mk.replay("S", scalar, sems)

        @block.gpsimd
        def _(gpsimd):
            mk.replay("G", gpsimd, sems)
    es.close()
    return nc


def prep_inputs(inputs):
    f = np.float32

    def c(a):
        return np.ascontiguousarray(np.asarray(a, dtype=f))

    x = c(inputs["L0_attn_x"])
    w_in = c(inputs["L0_attn_w_in"])
    w_conv = c(inputs["L0_attn_w_conv"])
    b_conv = c(inputs["L0_attn_b_conv"])
    w_x = c(inputs["L0_attn_w_x"])
    w_dt = c(inputs["L0_attn_w_dt"])
    b_dt = c(inputs["L0_attn_b_dt"])
    A_log = c(inputs["L0_attn_A_log"])
    D_p = c(inputs["L0_attn_D_param"])
    w_out = c(inputs["L0_attn_w_out"])
    w1 = c(inputs["L0_ffn_w1"])
    b1 = c(inputs["L0_ffn_b1"])
    w2 = c(inputs["L0_ffn_w2"])
    b2 = c(inputs["L0_ffn_b2"])
    cent = c(inputs["L0_ffn_centroids"])
    w_route = c(inputs["L0_ffn_w_route"])

    shared = {
        "w_in_r": np.ascontiguousarray(
            w_in.reshape(8, 128, 32, 128).transpose(2, 1, 0, 3)
            .reshape(32, 128, 1024)),
        "w_conv_r": np.ascontiguousarray(
            w_conv.reshape(NDI, 128, KC).transpose(1, 0, 2).reshape(128, 4 * NDI)),
        "b_conv_r": np.ascontiguousarray(b_conv.reshape(NDI, 128).T),
        "w_x_r": np.ascontiguousarray(
            w_x.reshape(NDI, 128, 96).transpose(1, 0, 2).reshape(128, NDI * 96)),
        "w_dt": w_dt,
        "b_dt_r": np.ascontiguousarray(b_dt.reshape(NDI, 128).T),
        "A_r": np.ascontiguousarray(
            A_log.reshape(NDI, 128, NS).transpose(1, 0, 2).reshape(128, NDI * NS)),
        "D_r": np.ascontiguousarray(D_p.reshape(NDI, 128).T),
        "w_out": w_out,
        "w1_r": np.ascontiguousarray(
            w1.reshape(E, 8, 128, 4, 128).transpose(0, 3, 2, 1, 4)
            .reshape(E, 4, 128, 1024)),
        "b1_r": np.ascontiguousarray(
            b1.reshape(E, 4, 128).transpose(2, 0, 1).reshape(128, E * 4)),
        "w2": w2,
        "b2": b2,
        "cent": cent,
        "w_route": w_route,
        "ident": np.eye(128, dtype=f),
    }
    in_maps = [dict(shared, x=np.ascontiguousarray(x[b])) for b in range(B)]
    return in_maps


def kernel(**inputs):
    in_maps = prep_inputs(inputs)
    nc = build_program(debug=_DEBUG)
    res = run_bass_kernel_spmd(nc, in_maps, core_ids=list(range(B)))
    out = np.stack([np.asarray(res.results[i]["out"]) for i in range(B)], axis=0)
    return out.astype(np.float32)


# version 17
# speedup vs baseline: 1.0298x; 1.0298x over previous
import contextlib
import numpy as np
import concourse.bass as bass
import concourse.mybir as mybir
from concourse.bass_utils import run_bass_kernel_spmd

F32 = mybir.dt.float32
AF = mybir.ActivationFunctionType
OP = mybir.AluOpType
AX = mybir.AxisListType.X

B, T, D = 8, 2048, 1024
DI, NS, RK = 2048, 16, 64
KC = 4
E, DE = 8, 512
EPS_LN = 1e-5

CH = 256
NCH = T // CH
NT = CH // 128
NDI = DI // 128
ND = D // 128
G = 4
NG = NS // G

_DEBUG = False
_SKIP = None
_SKIP_GATE = False


class Buf:
    __slots__ = ("ready", "readers", "name")

    def __init__(self, name=""):
        self.ready = None
        self.readers = []
        self.name = name


class MK:
    ENGS = ("Q", "T", "V", "S", "G")
    DSEMS = ("QR", "QX0", "QX1", "QX2", "QX3", "QW0", "QW1", "QO0", "QO1",
             "QA0", "QA1", "QB0", "QB1", "GB0", "GB1", "GR0", "GR1",
             "SB0", "QC0", "ST0", "ST1", "DBG")

    def __init__(self):
        self.cnt = {e: 0 for e in self.ENGS + self.DSEMS}
        self.ops = {e: [] for e in self.ENGS}
        self.waited = {e: {} for e in self.ENGS}
        self.pend_r = {e: [] for e in self.ENGS}
        self.pend_w = {e: [] for e in self.ENGS}

    def emit(self, eng, fn, reads=(), writes=(), inc=True, dma=None,
             vsafe=False):
        waits = {}

        def need(tok, war=False):
            if tok is None:
                return
            te, sem = tok[0], tok[1]
            if tok[3]:
                waits[sem] = max(waits.get(sem, 0), tok[2])
            elif te != eng:
                waits[sem] = max(waits.get(sem, 0), tok[2])
            elif not war and not tok[4] and eng in ("V", "G"):
                waits[sem] = max(waits.get(sem, 0), tok[2])

        for b in reads:
            need(b.ready)
        for b in writes:
            need(b.ready, war=True)
            for t in b.readers:
                need(t, war=True)
        for sem, v in waits.items():
            if self.waited[eng].get(sem, 0) < v:
                self.ops[eng].append(("w", sem, v))
                self.waited[eng][sem] = v
        if inc:
            sem = dma if dma else eng
            amt = 16 if dma else 1
            self.cnt[sem] += amt
            tok = (eng, sem, self.cnt[sem], dma is not None, vsafe)
            self.ops[eng].append(("e", fn, sem, amt))
            for b in list(writes) + self.pend_w[eng]:
                b.ready = tok
                b.readers = []
            for b in list(reads) + self.pend_r[eng]:
                b.readers.append(tok)
            self.pend_r[eng] = []
            self.pend_w[eng] = []
        else:
            self.ops[eng].append(("e", fn, None, 0))
            self.pend_r[eng] += list(reads)
            self.pend_w[eng] += list(writes)

    def final_wait(self, eng, sems_to_wait):
        for sem in sems_to_wait:
            self.ops[eng].append(("w", sem, self.cnt[sem]))

    def fence(self, eng, sem):
        if self.waited[eng].get(sem, 0) < self.cnt[sem]:
            self.ops[eng].append(("w", sem, self.cnt[sem]))
            self.waited[eng][sem] = self.cnt[sem]

    def replay(self, eng, obj, sems):
        for item in self.ops[eng]:
            if item[0] == "w":
                obj.wait_ge(sems[item[1]], item[2])
            else:
                inst = item[1](obj)
                if item[2] is not None:
                    inst.then_inc(sems[item[2]], item[3])


def seg3(buf):
    return buf[:].rearrange("p (n t) -> p n t", n=G)


def segcol(buf, t):
    return seg3(buf)[:, :, t:t + 1].squeeze(2)


def build_program(n_chunks=NCH, debug=False):
    nc = bass.Bass("TRN2", target_bir_lowering=False)

    dx = nc.dram_tensor("x", [T, D], F32, kind="ExternalInput")
    dwin = nc.dram_tensor("w_in_r", [32, 128, 1024], F32, kind="ExternalInput")
    dwcv = nc.dram_tensor("w_conv_r", [128, 4 * NDI], F32, kind="ExternalInput")
    dbcv = nc.dram_tensor("b_conv_r", [128, NDI], F32, kind="ExternalInput")
    dwx = nc.dram_tensor("w_x_r", [128, NDI * 96], F32, kind="ExternalInput")
    dwdt = nc.dram_tensor("w_dt", [RK, DI], F32, kind="ExternalInput")
    dbdt = nc.dram_tensor("b_dt_r", [128, NDI], F32, kind="ExternalInput")
    dA = nc.dram_tensor("A_r", [128, NDI * NS], F32, kind="ExternalInput")
    dD = nc.dram_tensor("D_r", [128, NDI], F32, kind="ExternalInput")
    dwout = nc.dram_tensor("w_out", [DI, D], F32, kind="ExternalInput")
    dw1 = nc.dram_tensor("w1_r", [E, 4, 128, 1024], F32, kind="ExternalInput")
    db1 = nc.dram_tensor("b1_r", [128, E * 4], F32, kind="ExternalInput")
    dw2 = nc.dram_tensor("w2", [E, DE, D], F32, kind="ExternalInput")
    db2 = nc.dram_tensor("b2", [E, D], F32, kind="ExternalInput")
    dcent = nc.dram_tensor("cent", [E, D], F32, kind="ExternalInput")
    dwrt = nc.dram_tensor("w_route", [E, D], F32, kind="ExternalInput")
    dident = nc.dram_tensor("ident", [128, 128], F32, kind="ExternalInput")
    dout = nc.dram_tensor("out", [T, D], F32, kind="ExternalOutput")
    if debug:
        d_xn = nc.dram_tensor("dbg_xnT", [D, T], F32, kind="ExternalOutput")
        d_u = nc.dram_tensor("dbg_uT", [DI, T], F32, kind="ExternalOutput")
        d_dt = nc.dram_tensor("dbg_dtT", [DI, T], F32, kind="ExternalOutput")
        d_y = nc.dram_tensor("dbg_yT", [DI, T], F32, kind="ExternalOutput")
        d_g = nc.dram_tensor("dbg_gates", [T, E], F32, kind="ExternalOutput")
        d_xd = nc.dram_tensor("dbg_xdbl", [96, T], F32, kind="ExternalOutput")
        d_y2 = nc.dram_tensor("dbg_y2T", [DI, T], F32, kind="ExternalOutput")

    def sb(name, p, f):
        return nc.alloc_sbuf_tensor(name, [p, f], F32)

    xsb = [sb(f"x_{tt}", 128, 1024) for tt in range(NT)]
    xres = sb("xres", 128, 1024)
    xn_t = sb("xn_t", 128, 1024)
    xnT = [[sb(f"xnT_{p}_{i}", 128, CH) for i in range(ND)] for p in range(2)]
    u_pad = [sb(f"upad_{i}", 128, CH + 4) for i in range(NDI)]
    uT = [sb(f"uT_{i}", 128, CH) for i in range(NDI)]
    dtb = [sb(f"dt_{i}", 128, CH) for i in range(NDI)]
    yT = [sb(f"yT_{i}", 128, CH) for i in range(NDI)]
    xdbl = sb("xdbl", 96, CH)
    spX = [sb("spX_0", 128, CH)]
    dAg = [sb(f"dAg_{p}", 128, G * CH) for p in range(2)]
    bbg = [sb(f"bbg_{p}", 128, G * CH) for p in range(3)]
    hhg = [sb(f"hhg_{p}", 128, G * CH) for p in range(2)]
    Bint = [sb(f"Bint_{p}", 128, G * CH) for p in range(2)]
    Cint = [sb(f"Cint_{p}", 128, G * CH) for p in range(2)]
    tmpd = [sb(f"tmpd_{p}", 128, G) for p in range(2)]
    hst = sb("hst", 128, NDI * NS)
    szb = [sb(f"sz_{j}", 128, CH) for j in range(2)]
    hgs = [sb(f"hgs_{m}", 128, CH) for m in range(2)]
    grep = [sb(f"grep_{j}", 128, CH) for j in range(2)]
    glog = [sb(f"glog_{tt}", 128, E) for tt in range(NT)]
    gm8 = [sb(f"gm8_{tt}", 128, E) for tt in range(NT)]
    gel = [sb(f"gel_{tt}", 128, E) for tt in range(NT)]
    gkp = [sb(f"gkp_{tt}", 128, E) for tt in range(NT)]
    gnm = [sb(f"gnm_{tt}", 128, 1) for tt in range(NT)]
    gsum = [sb(f"gsum_{tt}", 128, 1) for tt in range(NT)]
    gis = [sb(f"gis_{tt}", 128, 1) for tt in range(NT)]
    gatesT = [nc.alloc_sbuf_tensor(f"gatesT_{p}", [E, CH], F32) for p in range(2)]
    rstd = [sb(f"rstd_{tt}", 128, 1) for tt in range(NT)]
    nmr = [sb(f"nmr_{tt}", 128, 1) for tt in range(NT)]
    ssn = [sb(f"ssn_{tt}", 128, 1) for tt in range(NT)]
    invx = [sb(f"invx_{tt}", 128, 1) for tt in range(NT)]
    wincol = [sb(f"wincol_{j}", 128, 1024) for j in range(2)]
    woutr = [sb(f"woutr_{j}", 128, 1024) for j in range(2)]
    w1t = [sb(f"w1t_{j}", 128, 1024) for j in range(2)]
    w2t = [sb(f"w2t_{j}", 128, 1024) for j in range(2)]
    wdt_s = nc.alloc_sbuf_tensor("wdt_s", [RK, DI], F32)
    wx_s = sb("wx_s", 128, NDI * 96)
    Abuf = sb("Abuf", 128, NDI * NS)
    Dbuf = sb("Dbuf", 128, NDI)
    bcv_s = sb("bcv_s", 128, NDI)
    bdt_s = sb("bdt_s", 128, NDI)
    wcv_s = sb("wcv_s", 128, 4 * NDI)
    b1c_s = sb("b1c_s", 128, E * 4)
    b2_s = nc.alloc_sbuf_tensor("b2_s", [E, D], F32)
    cns = [nc.alloc_sbuf_tensor(f"cns_{j}", [E, 1], F32) for j in range(3)]
    cnrT = sb("cnrT", 128, ND * E)
    wrtT = sb("wrtT", 128, ND * E)
    ident_s = sb("ident_s", 128, 128)
    cst = sb("cst", 128, 3)

    psA = nc.alloc_psum_tensor("psA", [128, 512], F32)
    psF = [nc.alloc_psum_tensor(f"psF_{j}", [128, 512], F32) for j in range(2)]
    psO = [nc.alloc_psum_tensor(f"psO_{tt}", [128, 1024], F32) for tt in range(NT)]
    psZ = nc.alloc_psum_tensor("psZ", [128, 512], F32)

    mk = MK()
    b_x = [Buf(f"x{tt}") for tt in range(NT)]
    b_xres = Buf("xres")
    b_xn = Buf("xn")
    b_xnT = [[Buf(f"xnT{p}{i}") for i in range(ND)] for p in range(2)]
    b_upad = [Buf(f"upad{i}") for i in range(NDI)]
    b_uT = [Buf(f"uT{i}") for i in range(NDI)]
    b_dt = [Buf(f"dt{i}") for i in range(NDI)]
    b_yT = [Buf(f"yT{i}") for i in range(NDI)]
    b_xdbl = Buf("xdbl")
    b_spX = [Buf("spX0")]
    b_dAg = [Buf(f"dAg{p}") for p in range(2)]
    b_bbg = [Buf(f"bbg{p}") for p in range(3)]
    b_hhg = [Buf(f"hhg{p}") for p in range(2)]
    b_Bint = [Buf(f"Bint{p}") for p in range(2)]
    b_Cint = [Buf(f"Cint{p}") for p in range(2)]
    b_BintS = Buf("BintS")
    b_CintQ = Buf("CintQ")
    b_CintG = Buf("CintG")
    b_tmpd = [Buf(f"tmpd{p}") for p in range(2)]
    b_hst = [[Buf(f"hst{i}_{g}") for g in range(NG)] for i in range(NDI)]
    b_sz = [Buf(f"sz{j}") for j in range(2)]
    b_hgs = [Buf(f"hgs{m}") for m in range(2)]
    b_grep = [Buf(f"grep{j}") for j in range(2)]
    b_glog = [Buf(f"glog{tt}") for tt in range(NT)]
    b_gm8 = [Buf(f"gm8{tt}") for tt in range(NT)]
    b_gel = [Buf(f"gel{tt}") for tt in range(NT)]
    b_gkp = [Buf(f"gkp{tt}") for tt in range(NT)]
    b_gnm = [Buf(f"gnm{tt}") for tt in range(NT)]
    b_gsum = [Buf(f"gsum{tt}") for tt in range(NT)]
    b_gis = [Buf(f"gis{tt}") for tt in range(NT)]
    b_gatesT = [Buf(f"gatesT{p}") for p in range(2)]
    b_rstd = [Buf(f"rstd{tt}") for tt in range(NT)]
    b_nmr = [Buf(f"nmr{tt}") for tt in range(NT)]
    b_ssn = [Buf(f"ssn{tt}") for tt in range(NT)]
    b_invx = [Buf(f"invx{tt}") for tt in range(NT)]
    b_wincol = [Buf(f"wincol{j}") for j in range(2)]
    b_woutr = [Buf(f"woutr{j}") for j in range(2)]
    b_w1t = [Buf(f"w1t{j}") for j in range(2)]
    b_w2t = [Buf(f"w2t{j}") for j in range(2)]
    b_res = Buf("residents")
    b_cst = Buf("cst")
    b_psA = Buf("psA")
    b_psF = [Buf(f"psF{j}") for j in range(2)]
    b_psO = [Buf(f"psO{tt}") for tt in range(NT)]
    b_psZ = Buf("psZ")

    def ld(dst_ap, src_ap, bufs_w, sem="QR"):
        mk.emit("Q", lambda q, d=dst_ap, s=src_ap: q.dma_start(out=d, in_=s),
                reads=(), writes=bufs_w, dma=sem)

    ld(wdt_s[:], dwdt.ap(), [b_res])
    ld(wx_s[:], dwx.ap(), [b_res])
    ld(Abuf[:], dA.ap(), [b_res])
    ld(Dbuf[:], dD.ap(), [b_res])
    ld(bcv_s[:], dbcv.ap(), [b_res])
    ld(bdt_s[:], dbdt.ap(), [b_res])
    ld(wcv_s[:], dwcv.ap(), [b_res])
    ld(b1c_s[:], db1.ap(), [b_res])
    ld(b2_s[:], db2.ap(), [b_res])
    cent_s, wrt_s = w1t[0], w1t[1]
    ld(cent_s[0:E, :], dcent.ap(), [b_w1t[0]])
    ld(wrt_s[0:E, :], dwrt.ap(), [b_w1t[1]])
    ld(ident_s[:], dident.ap(), [b_res])

    mk.emit("V", lambda v: v.memset(cst[:, 0:1], EPS_LN), writes=[b_cst])
    mk.emit("V", lambda v: v.memset(cst[:, 1:2], -1.0), writes=[b_cst])
    mk.emit("V", lambda v: v.memset(cst[:, 2:3], 1.0 / 1024.0), writes=[b_cst])

    mk.emit("S", lambda s: s.activation(Abuf[:], Abuf[:], AF.Exp),
            reads=[b_res], writes=[b_res])
    mk.emit("V", lambda v: v.tensor_scalar_mul(Abuf[:], Abuf[:], cst[:, 1:2]),
            reads=[b_res, b_cst], writes=[b_res])
    mk.emit("V", lambda v: v.memset(hst[:], 0.0),
            writes=[bb for row in b_hst for bb in row])
    for p in range(2):
        mk.emit("V", lambda v, p=p: v.memset(dAg[p][:], 0.0), writes=[b_dAg[p]])
    for i in range(NDI):
        mk.emit("V", lambda v, i=i: v.memset(u_pad[i][:, 0:4], 0.0),
                writes=[b_upad[i]])
    mk.emit("S", lambda s: s.activation(w2t[0][0:E, :], cent_s[0:E, :],
                                        AF.Square, accum_out=cns[0][:]),
            reads=[b_w1t[0]], writes=[b_w2t[0], b_res])
    mk.emit("S", lambda s: s.activation(cns[1][:], cns[0][:], AF.Sqrt),
            reads=[b_res], writes=[b_res])
    mk.emit("V", lambda v: v.tensor_scalar_max(cns[1][:], cns[1][:], 1e-12),
            reads=[b_res], writes=[b_res])
    mk.emit("V", lambda v: v.reciprocal(cns[2][:], cns[1][:]),
            reads=[b_res], writes=[b_res])
    mk.emit("S", lambda s: s.mul(cent_s[0:E, :], cent_s[0:E, :], cns[2][:]),
            reads=[b_res, b_w1t[0]], writes=[b_w1t[0]])
    for srct, bsrc, dstT in ((cent_s, b_w1t[0], cnrT), (wrt_s, b_w1t[1], wrtT)):
        for j in range(ND):
            mk.emit("T", lambda t, s=srct, j=j: t.transpose(
                psA[:, 0:E], s[0:E, j * 128:(j + 1) * 128],
                ident_s[0:E, 0:E]), reads=[b_res, bsrc], writes=[b_psA])
            mk.emit("S", lambda s, d=dstT, j=j: s.copy(
                d[:, j * E:(j + 1) * E], psA[:, 0:E]),
                reads=[b_psA], writes=[b_res])

    def load_x(k):
        for tt in range(NT):
            r0 = k * CH + tt * 128
            ld(xsb[tt][:], dx.ap()[r0:r0 + 128, :], [b_x[tt]],
               sem=f"QX{tt}")

    load_x(0)

    scan_it = [0]
    NB_ = NDI // 4

    def bcast_g0_spread():
        for jj in range(2):
            mk.emit("G", lambda g_, jj=jj: g_.dma_start(
                out=Bint[0][:, jj * CH:(jj + 1) * CH],
                in_=xdbl[RK + jj:RK + jj + 1, :].unsqueeze(1)
                .to_broadcast([1, 128, CH])),
                reads=[b_xdbl], writes=[b_Bint[0]], dma="GB0")
            mk.emit("S", lambda s, jj=jj: s.dma_start(
                out=Bint[0][:, (jj + 2) * CH:(jj + 3) * CH],
                in_=xdbl[RK + jj + 2:RK + jj + 3, :].unsqueeze(1)
                .to_broadcast([1, 128, CH])),
                reads=[b_xdbl], writes=[b_BintS], dma="SB0")
        for jj in range(3):
            mk.emit("Q", lambda q, jj=jj: q.dma_start(
                out=Cint[0][:, jj * CH:(jj + 1) * CH],
                in_=xdbl[RK + NS + jj:RK + NS + jj + 1, :].unsqueeze(1)
                .to_broadcast([1, 128, CH])),
                reads=[b_xdbl], writes=[b_CintQ], dma="QC0")
        mk.emit("G", lambda g_: g_.dma_start(
            out=Cint[0][:, 3 * CH:4 * CH],
            in_=xdbl[RK + NS + 3:RK + NS + 4, :].unsqueeze(1)
            .to_broadcast([1, 128, CH])),
            reads=[b_xdbl], writes=[b_CintG], dma="GB0")

    def head_steps_early(kn):
        kp = kn % 2
        xnTk = xnT[kp]
        b_xnTk = b_xnT[kp]

        for tt in range(NT):
            xt = xsb[tt]
            bx = b_x[tt]
            mk.emit("V", lambda v, xt=xt, tt=tt: v.reduce_sum(
                ssn[tt][:], xt[:], AX), reads=[bx], writes=[b_ssn[tt]])
            mk.emit("V", lambda v, tt=tt: v.tensor_scalar_mul(
                nmr[tt][:], ssn[tt][:], -1.0 / 1024.0),
                reads=[b_ssn[tt]], writes=[b_nmr[tt]])
            yield
            mk.emit("S", lambda s, xt=xt, tt=tt: s.activation(
                xn_t[:], xt[:], AF.Identity, bias=nmr[tt][:]),
                reads=[bx, b_nmr[tt]], writes=[b_xn])
            mk.emit("S", lambda s, tt=tt: s.activation(
                xn_t[:], xn_t[:], AF.Square, accum_out=ssn[tt][:]),
                reads=[b_xn], writes=[b_xn, b_ssn[tt]])
            mk.emit("S", lambda s, tt=tt: s.activation(
                rstd[tt][:], ssn[tt][:], AF.Sqrt, bias=cst[:, 0:1],
                scale=cst[:, 2:3]),
                reads=[b_ssn[tt], b_cst], writes=[b_rstd[tt]])
            yield
            mk.emit("V", lambda v, tt=tt: v.reciprocal(rstd[tt][:], rstd[tt][:]),
                    reads=[b_rstd[tt]], writes=[b_rstd[tt]])
            mk.emit("V", lambda v, tt=tt: v.tensor_mul(
                nmr[tt][:], nmr[tt][:], rstd[tt][:]),
                reads=[b_nmr[tt], b_rstd[tt]], writes=[b_nmr[tt]])
            yield
            mk.emit("S", lambda s, xt=xt, tt=tt: s.activation(
                xn_t[:], xt[:], AF.Identity, bias=nmr[tt][:], scale=rstd[tt][:]),
                reads=[bx, b_nmr[tt], b_rstd[tt]], writes=[b_xn])
            if kn + 1 < n_chunks:
                r0 = (kn + 1) * CH + tt * 128
                ld(xsb[tt][:], dx.ap()[r0:r0 + 128, :], [b_x[tt]],
                   sem=f"QX{tt}")
            yield
            for i in range(ND):
                sl = i % 4
                mk.emit("T", lambda t, i=i, sl=sl: t.transpose(
                    psA[:, sl * 128:(sl + 1) * 128],
                    xn_t[:, i * 128:(i + 1) * 128], ident_s[:]),
                    reads=[b_xn, b_res], writes=[b_psA])
                mk.emit("S", lambda s, i=i, tt=tt, sl=sl: s.copy(
                    xnTk[i][:, tt * 128:(tt + 1) * 128],
                    psA[:, sl * 128:(sl + 1) * 128]),
                    reads=[b_psA], writes=[b_xnTk[i]])
                if i == 3:
                    yield
            yield
            mk.emit("S", lambda s, tt=tt: s.activation(
                xn_t[:], xn_t[:], AF.Square, accum_out=ssn[tt][:]),
                reads=[b_xn], writes=[b_xn, b_ssn[tt]])
            mk.emit("S", lambda s, tt=tt: s.activation(
                ssn[tt][:], ssn[tt][:], AF.Sqrt),
                reads=[b_ssn[tt]], writes=[b_ssn[tt]])
            mk.emit("V", lambda v, tt=tt: v.tensor_scalar_max(
                ssn[tt][:], ssn[tt][:], 1e-12),
                reads=[b_ssn[tt]], writes=[b_ssn[tt]])
            mk.emit("V", lambda v, tt=tt: v.reciprocal(invx[tt][:], ssn[tt][:]),
                    reads=[b_ssn[tt]], writes=[b_invx[tt]])
            yield

        if debug:
            c0n = kn * CH
            for i in range(ND):
                mk.emit("G", lambda g, i=i, c0n=c0n: g.dma_start(
                    out=d_xn.ap()[i * 128:(i + 1) * 128, c0n:c0n + CH],
                    in_=xnTk[i][:]), reads=[b_xnTk[i]], dma="DBG")

        for tt in range(NT):
            for kd in range(ND):
                mk.emit("T", lambda t, tt=tt, kd=kd: t.matmul(
                    psA[0:128, 0:E],
                    xnTk[kd][:, tt * 128:(tt + 1) * 128],
                    cnrT[:, kd * E:(kd + 1) * E],
                    start=(kd == 0), stop=(kd == ND - 1)),
                    reads=[b_xnTk[kd], b_res], writes=[b_psA],
                    inc=(kd == ND - 1))
            for kd in range(ND):
                mk.emit("T", lambda t, tt=tt, kd=kd: t.matmul(
                    psA[0:128, E:2 * E],
                    xnTk[kd][:, tt * 128:(tt + 1) * 128],
                    wrtT[:, kd * E:(kd + 1) * E],
                    start=(kd == 0), stop=(kd == ND - 1)),
                    reads=[b_xnTk[kd], b_res], writes=[b_psA],
                    inc=(kd == ND - 1))
            mk.emit("S", lambda s, tt=tt: s.copy(glog[tt][:], psA[:, 0:E]),
                    reads=[b_psA], writes=[b_glog[tt]])
            mk.emit("V", lambda v, tt=tt: v.scalar_tensor_tensor(
                glog[tt][:], glog[tt][:], invx[tt][:], psA[:, E:2 * E],
                OP.mult, OP.add),
                reads=[b_psA, b_glog[tt], b_invx[tt]], writes=[b_glog[tt]])
            mk.emit("V", lambda v, tt=tt: v.max(gm8[tt][:], glog[tt][:]),
                    reads=[b_glog[tt]], writes=[b_gm8[tt]])
            mk.emit("V", lambda v, tt=tt: v.tensor_scalar_mul(
                gnm[tt][:], gm8[tt][:, 0:1], -1.0),
                reads=[b_gm8[tt]], writes=[b_gnm[tt]])
            mk.emit("S", lambda s, tt=tt: s.activation(
                gel[tt][:], glog[tt][:], AF.Exp, bias=gnm[tt][:]),
                reads=[b_glog[tt], b_gnm[tt]], writes=[b_gel[tt]])
            mk.emit("V", lambda v, tt=tt: v.tensor_single_scalar(
                gkp[tt][:], glog[tt][:], gm8[tt][:, 1:2], OP.is_ge),
                reads=[b_glog[tt], b_gm8[tt]], writes=[b_gkp[tt]])
            mk.emit("V", lambda v, tt=tt: v.tensor_mul(
                gel[tt][:], gel[tt][:], gkp[tt][:]),
                reads=[b_gel[tt], b_gkp[tt]], writes=[b_gel[tt]])
            mk.emit("V", lambda v, tt=tt: v.reduce_sum(
                gsum[tt][:], gel[tt][:], AX),
                reads=[b_gel[tt]], writes=[b_gsum[tt]])
            mk.emit("V", lambda v, tt=tt: v.reciprocal(gis[tt][:], gsum[tt][:]),
                    reads=[b_gsum[tt]], writes=[b_gis[tt]])
            mk.emit("V", lambda v, tt=tt: v.tensor_scalar_mul(
                gel[tt][:], gel[tt][:], gis[tt][:]),
                reads=[b_gel[tt], b_gis[tt]], writes=[b_gel[tt]])
            if debug:
                c0n = kn * CH
                mk.emit("G", lambda g, tt=tt, c0n=c0n: g.dma_start(
                    out=d_g.ap()[c0n + tt * 128:c0n + (tt + 1) * 128, :],
                    in_=gel[tt][:]), reads=[b_gel[tt]], dma="DBG")
            mk.emit("T", lambda t, tt=tt: t.transpose(
                psA[0:E, 256 + tt * 128:256 + (tt + 1) * 128],
                gel[tt][:], ident_s[:]),
                reads=[b_gel[tt], b_res], writes=[b_psA])
            mk.emit("S", lambda s, tt=tt, kp=kp: s.copy(
                gatesT[kp][:, tt * 128:(tt + 1) * 128],
                psA[0:E, 256 + tt * 128:256 + (tt + 1) * 128]),
                reads=[b_psA], writes=[b_gatesT[kp]])
            yield


    def head_late_steps(kn):
        kp = kn % 2
        xnTk = xnT[kp]
        b_xnTk = b_xnT[kp]

        for m in range(NDI):
            j = m % 2
            ld(wincol[j][:], dwin.ap()[m, :, :], [b_wincol[j]], sem=f"QW{j}")
            for kd in range(8):
                mk.emit("T", lambda t, j=j, kd=kd: t.matmul(
                    psF[j][:, 0:CH],
                    wincol[j][:, kd * 128:(kd + 1) * 128],
                    xnTk[kd][:], start=(kd == 0), stop=(kd == 7)),
                    reads=[b_wincol[j], b_xnTk[kd]], writes=[b_psF[j]],
                    inc=(kd == 7))
            mk.emit("S", lambda s, m=m, j=j: s.copy(
                u_pad[m][:, 4:4 + CH], psF[j][:, 0:CH]),
                reads=[b_psF[j]], writes=[b_upad[m]])
            if m % 2 == 1:
                for mm in (m - 1, m):
                    mk.emit("V", lambda v, m=mm: v.tensor_scalar_mul(
                        uT[m][:], u_pad[m][:, 4:4 + CH],
                        wcv_s[:, m * 4 + 3:m * 4 + 4]),
                        reads=[b_upad[m], b_res], writes=[b_uT[m]],
                        vsafe=True)
                for kk in range(3):
                    for mm in (m - 1, m):
                        mk.emit("V", lambda v, m=mm, kk=kk:
                                v.scalar_tensor_tensor(
                            uT[m][:], u_pad[m][:, 1 + kk:1 + kk + CH],
                            wcv_s[:, m * 4 + kk:m * 4 + kk + 1], uT[m][:],
                            OP.mult, OP.add),
                            reads=[b_upad[m], b_res, b_uT[m]],
                            writes=[b_uT[m]], vsafe=True)
                for mm in (m - 1, m):
                    mk.emit("S", lambda s, m=mm: s.copy(
                        u_pad[m][:, 0:4], u_pad[m][:, CH:CH + 4]),
                        reads=[b_upad[m]], writes=[b_upad[m]])
                    mk.emit("S", lambda s, m=mm: s.activation(
                        uT[m][:], uT[m][:], AF.Silu, bias=bcv_s[:, m:m + 1]),
                        reads=[b_uT[m], b_res], writes=[b_uT[m]])
            yield

        for kd in range(NDI):
            mk.emit("T", lambda t, kd=kd: t.matmul(
                psZ[0:96, 0:CH], wx_s[:, kd * 96:(kd + 1) * 96],
                uT[kd][:], start=(kd == 0), stop=(kd == NDI - 1)),
                reads=[b_res, b_uT[kd]], writes=[b_psZ],
                inc=(kd == NDI - 1))
        mk.emit("S", lambda s: s.copy(xdbl[:], psZ[0:96, 0:CH]),
                reads=[b_psZ], writes=[b_xdbl])
        if debug:
            c0n = kn * CH
            mk.emit("G", lambda g, c0n=c0n: g.dma_start(
                out=d_xd.ap()[:, c0n:c0n + CH], in_=xdbl[:]),
                reads=[b_xdbl], dma="DBG")

        for i in range(NDI):
            j = i % 2
            mk.emit("T", lambda t, i=i, j=j: t.matmul(
                psF[j][:, 0:CH], wdt_s[:, i * 128:(i + 1) * 128],
                xdbl[0:RK, :], start=True, stop=True),
                reads=[b_res, b_xdbl], writes=[b_psF[j]])
            mk.emit("S", lambda s, i=i, j=j: s.activation(
                spX[0][:], psF[j][:, 0:CH], AF.Abs, bias=bdt_s[:, i:i + 1]),
                reads=[b_psF[j], b_res], writes=[b_spX[0]])
            mk.emit("S", lambda s: s.activation(
                spX[0][:], spX[0][:], AF.Exp, scale=cst[:, 1:2]),
                reads=[b_spX[0], b_cst], writes=[b_spX[0]])
            mk.emit("S", lambda s: s.activation(
                spX[0][:], spX[0][:], AF.Ln, bias=1.0),
                reads=[b_spX[0]], writes=[b_spX[0]])
            mk.emit("S", lambda s, i=i, j=j: s.activation(
                dtb[i][:], psF[j][:, 0:CH], AF.Relu, bias=bdt_s[:, i:i + 1]),
                reads=[b_psF[j], b_res], writes=[b_dt[i]])
            mk.emit("V", lambda v, i=i: v.tensor_add(
                dtb[i][:], dtb[i][:], spX[0][:]),
                reads=[b_spX[0], b_dt[i]], writes=[b_dt[i]], vsafe=True)
        for i in range(NDI):
            mk.emit("V", lambda v, i=i: v.tensor_mul(
                u_pad[i][:, 4:4 + CH], dtb[i][:], uT[i][:]),
                reads=[b_dt[i], b_uT[i]], writes=[b_upad[i]], vsafe=True)
        yield
        bcast_g0_spread()

        if debug:
            c0n = kn * CH
            for i in range(NDI):
                mk.emit("G", lambda g, i=i, c0n=c0n: g.dma_start(
                    out=d_u.ap()[i * 128:(i + 1) * 128, c0n:c0n + CH],
                    in_=uT[i][:]), reads=[b_uT[i]], dma="DBG")
                mk.emit("G", lambda g, i=i, c0n=c0n: g.dma_start(
                    out=d_dt.ap()[i * 128:(i + 1) * 128, c0n:c0n + CH],
                    in_=dtb[i][:]), reads=[b_dt[i]], dma="DBG")


    def emit_scanphase(k, feeder, late_feeder, bcast0_done):
        kp = k % 2
        c0 = k * CH
        xnTk = xnT[kp]
        b_xnTk = b_xnT[kp]

        def feed(n):
            for _ in range(n):
                if next(feeder, "END") == "END":
                    break

        def ffn_stages(e):
            h = e % 2
            mk.emit("G", lambda g, e=e, h=h, kp=kp: g.dma_start(
                out=grep[h][:],
                in_=gatesT[kp][e:e + 1, :].unsqueeze(1)
                .to_broadcast([1, 128, CH])),
                reads=[b_gatesT[kp]], writes=[b_grep[h]], dma=f"GR{h}")

            def h_stage(m):
                j = (e * 4 + m) % 2
                hj = m % 2
                ld(w1t[j][:], dw1.ap()[e, m, :, :], [b_w1t[j]], sem=f"QA{j}")
                for kd in range(8):
                    mk.emit("T", lambda t, j=j, kd=kd: t.matmul(
                        psF[j][:, 0:CH], w1t[j][:, kd * 128:(kd + 1) * 128],
                        xnTk[kd][:], start=(kd == 0), stop=(kd == 7)),
                        reads=[b_w1t[j], b_xnTk[kd]], writes=[b_psF[j]],
                        inc=(kd == 7))
                mk.emit("S", lambda s, e=e, m=m, j=j, hj=hj: s.activation(
                    hgs[hj][:], psF[j][:, 0:CH], AF.Gelu,
                    bias=b1c_s[:, e * 4 + m:e * 4 + m + 1]),
                    reads=[b_psF[j], b_res], writes=[b_hgs[hj]])
                mk.emit("G", lambda g_, hj=hj, h=h: g_.tensor_tensor(
                    hgs[hj][:], hgs[hj][:], grep[h][:], OP.mult),
                    reads=[b_hgs[hj], b_grep[h]], writes=[b_hgs[hj]])

            def w2_stage(m):
                wj = m % 2
                hj = m % 2
                ld(w2t[wj][:], dw2.ap()[e, m * 128:(m + 1) * 128, :],
                   [b_w2t[wj]], sem=f"QB{wj}")
                for tt in range(NT):
                    for nh in range(2):
                        mk.emit("T", lambda t, wj=wj, hj=hj, tt=tt, nh=nh,
                                e=e, m=m: t.matmul(
                            psO[tt][:, nh * 512:(nh + 1) * 512],
                            hgs[hj][:, tt * 128:(tt + 1) * 128],
                            w2t[wj][:, nh * 512:(nh + 1) * 512],
                            start=(e == 0 and m == 0), stop=False),
                            reads=[b_w2t[wj], b_hgs[hj]], writes=[b_psO[tt]],
                            inc=(tt == NT - 1 and nh == 1))

            return [lambda m=m: h_stage(m) if m >= 0 else w2_stage(-1 - m)
                    for m in (0, 1, -1, 2, -2, 3, -3, -4)]

        pend = [None, None]

        def emit_scan_core(i, g, pp, ap2, gp):
            mk.emit("V", lambda v, pp=pp, ap2=ap2: v.tensor_tensor_scan(
                hhg[ap2][:], dAg[ap2][:], bbg[pp][:], 0.0, OP.mult, OP.add),
                reads=[b_dAg[ap2], b_bbg[pp]], writes=[b_hhg[ap2]],
                vsafe=True)
            mk.emit("G", lambda g_, i=i, g=g, ap2=ap2: g_.tensor_copy(
                hst[:, i * NS + g * G:i * NS + (g + 1) * G],
                segcol(hhg[ap2], CH - 1)),
                reads=[b_hhg[ap2]], writes=[b_hst[i][g]])

        def emit_ytail(i, g, pp, ap2, gp):
            mk.emit("V", lambda v, pp=pp, ap2=ap2, gp=gp: v.tensor_tensor(
                bbg[pp][:], hhg[ap2][:], Cint[gp][:], OP.mult),
                reads=[b_hhg[ap2], b_Cint[gp]]
                + ([b_CintQ, b_CintG] if gp == 0 else []),
                writes=[b_bbg[pp]], vsafe=True)
            mk.emit("V", lambda v, pp=pp: v.tensor_add(
                bbg[pp][:, 0:2 * CH], bbg[pp][:, 0:2 * CH],
                bbg[pp][:, 2 * CH:4 * CH]),
                reads=[b_bbg[pp]], writes=[b_bbg[pp]], vsafe=True)
            if g == 0:
                mk.emit("V", lambda v, i=i, pp=pp: v.tensor_add(
                    yT[i][:], bbg[pp][:, 0:CH], bbg[pp][:, CH:2 * CH]),
                    reads=[b_bbg[pp]], writes=[b_yT[i]], vsafe=True)
            else:
                mk.emit("V", lambda v, pp=pp: v.tensor_add(
                    bbg[pp][:, 0:CH], bbg[pp][:, 0:CH], bbg[pp][:, CH:2 * CH]),
                    reads=[b_bbg[pp]], writes=[b_bbg[pp]], vsafe=True)
                mk.emit("V", lambda v, i=i, pp=pp: v.tensor_add(
                    yT[i][:], yT[i][:], bbg[pp][:, 0:CH]),
                    reads=[b_bbg[pp], b_yT[i]], writes=[b_yT[i]], vsafe=True)

        def advance(cur):
            if pend[1] is not None:
                emit_ytail(*pend[1])
            pend[1] = pend[0]
            pend[0] = cur

        ffn_iter = iter(range(E))

        def zy2(i):
            m = NDI + i
            j = m % 2
            ld(wincol[j][:], dwin.ap()[m, :, :], [b_wincol[j]], sem=f"QW{j}")
            for kd in range(8):
                mk.emit("T", lambda t, j=j, kd=kd: t.matmul(
                    psZ[:, 0:CH], wincol[j][:, kd * 128:(kd + 1) * 128],
                    xnTk[kd][:], start=(kd == 0), stop=(kd == 7)),
                    reads=[b_wincol[j], b_xnTk[kd]], writes=[b_psZ],
                    inc=(kd == 7))
            mk.emit("S", lambda s, i=i: s.activation(
                szb[i % 2][:], psZ[:, 0:CH], AF.Silu),
                reads=[b_psZ], writes=[b_sz[i % 2]])
            mk.emit("V", lambda v, i=i: v.scalar_tensor_tensor(
                yT[i][:], uT[i][:], Dbuf[:, i:i + 1], yT[i][:],
                OP.mult, OP.add),
                reads=[b_uT[i], b_res, b_yT[i]], writes=[b_yT[i]],
                vsafe=True)
            mk.emit("V", lambda v, i=i: v.tensor_mul(
                yT[i][:], yT[i][:], szb[i % 2][:]),
                reads=[b_yT[i], b_sz[i % 2]], writes=[b_yT[i]], vsafe=True)

        zy2_done = [0]

        def feed_late(n):
            for _ in range(n):
                if next(late_feeder, "END") == "END":
                    break

        def bc_bcast(g):
            gp = g % 2
            for jj in range(G):
                n = g * G + jj
                mk.emit("G", lambda g_, n=n, jj=jj, gp=gp: g_.dma_start(
                    out=Bint[gp][:, jj * CH:(jj + 1) * CH],
                    in_=xdbl[RK + n:RK + n + 1, :].unsqueeze(1)
                    .to_broadcast([1, 128, CH])),
                    reads=[b_xdbl], writes=[b_Bint[gp]], dma=f"GB{gp}")
                mk.emit("G", lambda g_, n=n, jj=jj, gp=gp: g_.dma_start(
                    out=Cint[gp][:, jj * CH:(jj + 1) * CH],
                    in_=xdbl[RK + NS + n:RK + NS + n + 1, :].unsqueeze(1)
                    .to_broadcast([1, 128, CH])),
                    reads=[b_xdbl], writes=[b_Cint[gp]], dma=f"GB{gp}")

        stageq = []

        for g in range(NG):
            gp = g % 2
            if g == 0 and not bcast0_done:
                bcast_g0_spread()
            for _ in range(2):
                e = next(ffn_iter, None)
                if e is not None:
                    stageq.extend(ffn_stages(e))
            for i in range(NDI):
                pp = scan_it[0] % 3
                ap2 = scan_it[0] % 2
                scan_it[0] += 1
                cur = (i, g, pp, ap2, gp)
                for jj in range(G):
                    n = g * G + jj
                    mk.emit("S", lambda s, i=i, jj=jj, n=n, ap2=ap2:
                            s.activation(
                        seg3(dAg[ap2])[:, jj:jj + 1, 1:CH].squeeze(1),
                        dtb[i][:, 1:CH], AF.Exp,
                        scale=Abuf[:, i * NS + n:i * NS + n + 1]),
                        reads=[b_dt[i], b_res], writes=[b_dAg[ap2]],
                        inc=(jj == G - 1))
                tp = scan_it[0] % 2
                mk.emit("S", lambda s, i=i, g=g, tp=tp: s.mul(
                    tmpd[tp][:],
                    Abuf[:, i * NS + g * G:i * NS + (g + 1) * G],
                    dtb[i][:, 0:1]),
                    reads=[b_dt[i], b_res], writes=[b_tmpd[tp]])
                mk.emit("S", lambda s, tp=tp: s.activation(
                    tmpd[tp][:], tmpd[tp][:], AF.Exp),
                    reads=[b_tmpd[tp]], writes=[b_tmpd[tp]])
                mk.emit("G", lambda g_, i=i, g=g, tp=tp: g_.tensor_tensor(
                    tmpd[tp][:], tmpd[tp][:],
                    hst[:, i * NS + g * G:i * NS + (g + 1) * G], OP.mult),
                    reads=[b_tmpd[tp], b_hst[i][g]], writes=[b_tmpd[tp]])
                mk.emit("V", lambda v, i=i, pp=pp, gp=gp: v.tensor_tensor(
                    seg3(bbg[pp]),
                    u_pad[i][:, 4:4 + CH].unsqueeze(1)
                    .to_broadcast([128, G, CH]),
                    seg3(Bint[gp]), OP.mult),
                    reads=[b_upad[i], b_Bint[gp]]
                    + ([b_BintS] if gp == 0 else []),
                    writes=[b_bbg[pp]], vsafe=True)
                mk.emit("G", lambda g_, pp=pp, tp=tp: g_.tensor_tensor(
                    segcol(bbg[pp], 0), segcol(bbg[pp], 0), tmpd[tp][:],
                    OP.add),
                    reads=[b_tmpd[tp], b_bbg[pp]], writes=[b_bbg[pp]])
                if stageq:
                    stageq.pop(0)()
                if pend[0] is not None:
                    emit_scan_core(*pend[0])
                advance(cur)
                if i == 2 and g + 1 < NG:
                    bc_bcast(g + 1)
                if g < NG - 1:
                    if i % 2 == 1:
                        feed(1)
                else:
                    if i >= 2:
                        zy2(i - 2)
                        zy2_done[0] = i - 1
                        if i % 2 == 1:
                            feed_late(1)
        if pend[0] is not None:
            emit_scan_core(*pend[0])
        if pend[1] is not None:
            emit_ytail(*pend[1])
        if pend[0] is not None:
            emit_ytail(*pend[0])
        for e in ffn_iter:
            stageq.extend(ffn_stages(e))
        for th in stageq:
            th()
        stageq.clear()
        for i in range(zy2_done[0], NDI):
            zy2(i)
        feed(1000)
        feed_late(1000)

        if debug:
            for i in range(NDI):
                mk.emit("G", lambda g, i=i, c0=c0: g.dma_start(
                    out=d_y2.ap()[i * 128:(i + 1) * 128, c0:c0 + CH],
                    in_=yT[i][:]), reads=[b_yT[i]], dma="DBG")

    def emit_tail(k):
        par = k % 2
        kp = k % 2
        c0 = k * CH

        for kd in range(NDI):
            j = kd % 2
            ld(woutr[j][:], dwout.ap()[kd * 128:(kd + 1) * 128, :],
               [b_woutr[j]], sem=f"QO{j}")
            for tt in range(NT):
                for nh in range(2):
                    mk.emit("T", lambda t, j=j, kd=kd, tt=tt, nh=nh: t.matmul(
                        psO[tt][:, nh * 512:(nh + 1) * 512],
                        yT[kd][:, tt * 128:(tt + 1) * 128],
                        woutr[j][:, nh * 512:(nh + 1) * 512],
                        start=False, stop=False),
                        reads=[b_woutr[j], b_yT[kd]], writes=[b_psO[tt]],
                        inc=(tt == NT - 1 and nh == 1))

        for tt in range(NT):
            for nh in range(2):
                mk.emit("T", lambda t, tt=tt, nh=nh, kp=kp: t.matmul(
                    psO[tt][:, nh * 512:(nh + 1) * 512],
                    gatesT[kp][:, tt * 128:(tt + 1) * 128],
                    b2_s[:, nh * 512:(nh + 1) * 512],
                    start=False, stop=True),
                    reads=[b_gatesT[kp], b_res], writes=[b_psO[tt]],
                    inc=(nh == 1))

        for tt in range(NT):
            r0 = c0 + tt * 128
            ld(xres[:], dx.ap()[r0:r0 + 128, :], [b_xres], sem="QX2")
            mk.emit("V", lambda v, tt=tt: v.tensor_add(
                xres[:], psO[tt][:], xres[:]),
                reads=[b_psO[tt], b_xres], writes=[b_xres], vsafe=True)
            mk.emit("G", lambda g, tt=tt, c0=c0: g.dma_start(
                out=dout.ap()[c0 + tt * 128:c0 + (tt + 1) * 128, :],
                in_=xres[:]), reads=[b_xres], dma=f"ST{k % 2}")

    for _ in head_steps_early(0):
        pass
    for _ in head_late_steps(0):
        pass
    for k in range(n_chunks):
        feeder = head_steps_early(k + 1) if k + 1 < n_chunks else iter(())
        late_feeder = head_late_steps(k + 1) if k + 1 < n_chunks else iter(())
        emit_scanphase(k, feeder, late_feeder, bcast0_done=(k > 0))
        emit_tail(k)

    mk.final_wait("G", ["ST0", "ST1", "DBG"])

    es = contextlib.ExitStack()
    sems = {e: es.enter_context(nc.semaphore(f"sem{e}"))
            for e in MK.ENGS + MK.DSEMS}
    with nc.Block() as block:
        @block.sync
        def _(sync):
            mk.replay("Q", sync, sems)

        @block.tensor
        def _(tensor):
            mk.replay("T", tensor, sems)

        @block.vector
        def _(vector):
            mk.replay("V", vector, sems)

        @block.scalar
        def _(scalar):
            mk.replay("S", scalar, sems)

        @block.gpsimd
        def _(gpsimd):
            mk.replay("G", gpsimd, sems)
    es.close()
    return nc


def prep_inputs(inputs):
    f = np.float32

    def c(a):
        return np.ascontiguousarray(np.asarray(a, dtype=f))

    x = c(inputs["L0_attn_x"])
    w_in = c(inputs["L0_attn_w_in"])
    w_conv = c(inputs["L0_attn_w_conv"])
    b_conv = c(inputs["L0_attn_b_conv"])
    w_x = c(inputs["L0_attn_w_x"])
    w_dt = c(inputs["L0_attn_w_dt"])
    b_dt = c(inputs["L0_attn_b_dt"])
    A_log = c(inputs["L0_attn_A_log"])
    D_p = c(inputs["L0_attn_D_param"])
    w_out = c(inputs["L0_attn_w_out"])
    w1 = c(inputs["L0_ffn_w1"])
    b1 = c(inputs["L0_ffn_b1"])
    w2 = c(inputs["L0_ffn_w2"])
    b2 = c(inputs["L0_ffn_b2"])
    cent = c(inputs["L0_ffn_centroids"])
    w_route = c(inputs["L0_ffn_w_route"])

    shared = {
        "w_in_r": np.ascontiguousarray(
            w_in.reshape(8, 128, 32, 128).transpose(2, 1, 0, 3)
            .reshape(32, 128, 1024)),
        "w_conv_r": np.ascontiguousarray(
            w_conv.reshape(NDI, 128, KC).transpose(1, 0, 2).reshape(128, 4 * NDI)),
        "b_conv_r": np.ascontiguousarray(b_conv.reshape(NDI, 128).T),
        "w_x_r": np.ascontiguousarray(
            w_x.reshape(NDI, 128, 96).transpose(1, 0, 2).reshape(128, NDI * 96)),
        "w_dt": w_dt,
        "b_dt_r": np.ascontiguousarray(b_dt.reshape(NDI, 128).T),
        "A_r": np.ascontiguousarray(
            A_log.reshape(NDI, 128, NS).transpose(1, 0, 2).reshape(128, NDI * NS)),
        "D_r": np.ascontiguousarray(D_p.reshape(NDI, 128).T),
        "w_out": w_out,
        "w1_r": np.ascontiguousarray(
            w1.reshape(E, 8, 128, 4, 128).transpose(0, 3, 2, 1, 4)
            .reshape(E, 4, 128, 1024)),
        "b1_r": np.ascontiguousarray(
            b1.reshape(E, 4, 128).transpose(2, 0, 1).reshape(128, E * 4)),
        "w2": w2,
        "b2": b2,
        "cent": cent,
        "w_route": w_route,
        "ident": np.eye(128, dtype=f),
    }
    in_maps = [dict(shared, x=np.ascontiguousarray(x[b])) for b in range(B)]
    return in_maps


def kernel(**inputs):
    in_maps = prep_inputs(inputs)
    nc = build_program(debug=_DEBUG)
    res = run_bass_kernel_spmd(nc, in_maps, core_ids=list(range(B)))
    out = np.stack([np.asarray(res.results[i]["out"]) for i in range(B)], axis=0)
    return out.astype(np.float32)
